# version 02 (final)
# baseline (speedup 1.0000x reference)
import numpy as np
from contextlib import ExitStack
import concourse.bass as bass
import concourse.mybir as mybir
from concourse.bass_types import AP
from concourse.bass_utils import run_bass_kernel_spmd

F32 = mybir.dt.float32
BF16 = mybir.dt.bfloat16
AF = mybir.ActivationFunctionType
OP = mybir.AluOpType

C = 2048; TLOC = 1024; TFULL = 2048; NCT = 16; EPS = 1e-5
ENGS = ("sync", "scalar", "vector", "tensor", "gpsimd")
_UID = [0]
def uname(n):
    _UID[0] += 1
    return f"{n}_u{_UID[0]}" 

class P:
    def __init__(self, nc, stack, block):
        self.nc = nc; self.stack = stack; self.block = block
        self.sems = {}; self.cnt = {}
        for name in ("dS","dA","dG","dV","pe","act","dve","pool"):
            self.sems[name] = stack.enter_context(nc.semaphore(name))
            self.cnt[name] = 0
        self.snap = dict(self.cnt)
        self.last_seen = {e: {s: 0 for s in self.sems} for e in ENGS}
        self.nstages = 0

    def stage(self):
        self.snap = dict(self.cnt)
        self.nstages += 1
        return self

    def on(self, eng, fn):
        dec = getattr(self.block, eng)
        snap = self.snap
        ls = self.last_seen[eng]
        def body(e):
            for s, v in snap.items():
                if v > ls[s]:
                    e.wait_ge(self.sems[s], v)
                    ls[s] = v
            fn(e)
        dec(body)
        return self

    def ser(self, e, name):
        e.wait_ge(self.sems[name], self.cnt[name])

    def inc(self, instr, name, k=1):
        instr.then_inc(self.sems[name], k)
        self.cnt[name] += k

    def finish(self):
        self.stage()
        def fin(e):
            for s, v in self.cnt.items():
                if v > self.last_seen["sync"][s]:
                    e.wait_ge(self.sems[s], v)
                    self.last_seen["sync"][s] = v
        self.block.sync(fin)

def dma(p, e, q, out, in_, k=16):
    i = e.dma_start(out=out, in_=in_)
    p.inc(i, q, k)

def bcast_ap(src_ap, dim, count):
    a = list(src_ap.ap)
    a.insert(dim, [0, count])
    return AP(src_ap.tensor, src_ap.offset, a)

def mm(p, e, out, lhsT, rhs, start=True, stop=True, inc=True):
    i = e.matmul(out, lhsT, rhs, start=start, stop=stop)
    if inc: p.inc(i, "pe", 1)
    return i

def wavepipe(p, nunits, emit_mm, emit_ev, nbanks=4, mm_eng="tensor", ev_eng="vector", ev_sem=None):
    if ev_sem is None:
        ev_sem = "dve" if ev_eng == "vector" else "act"
    base_pe = p.cnt["pe"]
    base_ev = p.cnt[ev_sem]
    p.stage()
    def fmm(e):
        for u in range(nunits):
            if u >= nbanks:
                e.wait_ge(p.sems[ev_sem], base_ev + (u - nbanks + 1))
            emit_mm(e, u, u % nbanks)
    p.on(mm_eng, fmm)
    def fev(e):
        for u in range(nunits):
            e.wait_ge(p.sems["pe"], base_pe + u + 1)
            emit_ev(e, u, u % nbanks)
    p.on(ev_eng, fev)
    assert p.cnt["pe"] == base_pe + nunits, f"emit_mm must inc pe once per unit ({p.cnt['pe']} vs {base_pe}+{nunits})"
    assert p.cnt[ev_sem] == base_ev + nunits, "emit_ev must inc ev_sem once per unit"

def emit_ln_stream(p, ctx, src_dram, out_dram, add_dram=None, writeback=None, tw=2048, passes=3):
    nc = p.nc
    PS = ctx['PS']; rows = ctx['rows']
    XB, TB = ctx['XB'], ctx['TB']
    mean_sb, rstd_sb = ctx['mean_sb'], ctx['rstd_sb']
    on1, on1f = ctx['ones128_1'], ctx['ones1_128']
    nq = tw // 512
    assert nq <= 4
    for i in range(NCT):
        sl = i % 2
        p.stage()
        def ld(e, i=i, sl=sl):
            dma(p, e, "dS", XB[sl][:, :], src_dram[i*128:(i+1)*128, :])
            if add_dram is not None:
                dma(p, e, "dA", ctx['ADB'][sl][:, :], add_dram[i*128:(i+1)*128, :])
        p.on("sync", ld)
        if add_dram is not None:
            p.stage()
            def ad(e, i=i, sl=sl):
                ii = e.tensor_tensor(out=XB[sl][:, :], in0=XB[sl][:, :], in1=ctx['ADB'][sl][:, :], op=OP.add)
                p.inc(ii, "dve", 1)
            p.on("vector", ad)
            def wb(e, i=i, sl=sl):
                if writeback is not None:
                    pass
            if writeback is not None:
                p.stage()
                def wb2(e, i=i, sl=sl):
                    dma(p, e, "dA", writeback[i*128:(i+1)*128, :], XB[sl][:, :])
                p.on("scalar", wb2)
        if True:
            p.stage()
            def cst(e, sl=sl):
                ii = e.tensor_copy(out=TB[sl][:, :], in_=XB[sl][:, :])
                p.inc(ii, "dve", 1)
            p.on("vector", cst)
        if True:
            p.stage()
            def smm(e, i=i, sl=sl):
                for q in range(nq):
                    mm(p, e, PS[q][0:1, :], on1[:, 0:1], TB[sl][:, q*512:(q+1)*512],
                       start=(i==0), stop=(i==NCT-1), inc=(q==nq-1))
            p.on("tensor", smm)

    p.stage()
    def mrow(e):
        for q in range(nq):
            ii = e.activation(out=rows[0][:, q*512:(q+1)*512], in_=PS[q][0:1, :], func=AF.Copy, scale=1.0/C)
        p.inc(ii, "act", 1)
    p.on("scalar", mrow)

    p.stage()
    def mbc(e):
        for q in range(nq):
            mm(p, e, PS[q][:, :], on1f[0:1, :], rows[0][:, q*512:(q+1)*512])
    p.on("tensor", mbc)
    p.stage()
    def mev(e):
        for q in range(nq):
            ii = e.tensor_copy(out=mean_sb[:, q*512:(q+1)*512], in_=PS[q][:, :])
        p.inc(ii, "dve", 1)
    p.on("vector", mev)
    if passes < 2: return
    for i in range(NCT):
        sl = i % 2
        p.stage()
        def ld2(e, i=i, sl=sl):
            src = writeback if (add_dram is not None and writeback is not None) else src_dram
            dma(p, e, "dS", XB[sl][:, :], src[i*128:(i+1)*128, :])
        p.on("sync", ld2)
        p.stage()
        def sub(e, sl=sl):
            ii = e.tensor_tensor(out=TB[sl][:, :], in0=XB[sl][:, :], in1=mean_sb[:, :], op=OP.subtract)
            p.inc(ii, "dve", 1)
        p.on("vector", sub)
        p.stage()
        def sq2(e, sl=sl):
            ii = e.activation(out=ctx['SQB'][:, :], in_=TB[sl][:, :], func=AF.Square)
            p.inc(ii, "act", 1)
        p.on("scalar", sq2)
        p.stage()
        def smm2(e, i=i, sl=sl):
            for q in range(nq):
                mm(p, e, PS[q][0:1, :], on1[:, 0:1], ctx['SQB'][:, q*512:(q+1)*512],
                   start=(i==0), stop=(i==NCT-1), inc=(q==nq-1))
        p.on("tensor", smm2)
        def st2(e, i=i, sl=sl):
            dma(p, e, "dA", out_dram[i*128:(i+1)*128, :], TB[sl][:, :])
        p.on("scalar", st2)
    p.stage()
    def vrow(e):
        for q in range(nq):
            ii = e.activation(out=rows[1][:, q*512:(q+1)*512], in_=PS[q][0:1, :], func=AF.Copy, scale=1.0/C)
        p.inc(ii, "act", 1)
    p.on("scalar", vrow)
    p.stage()
    def rrow(e):
        i1 = e.activation(out=rows[2][:, :], in_=rows[1][:, :], func=AF.Ln, bias=ctx['eps1'][:, :])
        p.inc(i1, "act", 1); p.ser(e, "act")
        ii = e.activation(out=rows[1][:, :], in_=rows[2][:, :], func=AF.Exp, scale=-0.5)
        p.inc(ii, "act", 1)
    p.on("scalar", rrow)
    p.stage()
    def rbc(e):
        for q in range(nq):
            mm(p, e, PS[q][:, :], on1f[0:1, :], rows[1][:, q*512:(q+1)*512])
    p.on("tensor", rbc)
    p.stage()
    def rev(e):
        for q in range(nq):
            ii = e.tensor_copy(out=rstd_sb[:, q*512:(q+1)*512], in_=PS[q][:, :])
        p.inc(ii, "dve", 1)
    p.on("vector", rev)
    if passes < 3: return
    for i in range(NCT):
        sl = i % 2
        p.stage()
        def ld3(e, i=i, sl=sl):
            dma(p, e, "dS", TB[sl][:, :], out_dram[i*128:(i+1)*128, :])
        p.on("sync", ld3)
        p.stage()
        def ml3(e, sl=sl):
            ii = e.tensor_tensor(out=TB[sl][:, :], in0=TB[sl][:, :], in1=rstd_sb[:, :], op=OP.mult)
            p.inc(ii, "dve", 1)
        p.on("vector", ml3)
        p.stage()
        def st3(e, i=i, sl=sl):
            dma(p, e, "dA", out_dram[i*128:(i+1)*128, :], TB[sl][:, :])
        p.on("scalar", st3)

"""attn_a phase: head-split across pair, full T=2048 per core."""

TF = 2048
HCT = 16
L = 64
NJ = TF // L


def tp128(p, e, PST, identbf, src_ap, slot):
    i = e.transpose(PST[:, slot*128:(slot+1)*128], src_ap, identbf[:, :])
    p.inc(i, "pe", 1)

def fr3(ap_, nblk, blklen):
    a = list(ap_.ap)
    assert a[-1][1] == nblk*blklen
    a = a[:-1] + [[blklen, nblk], [a[-1][0], blklen]]
    return AP(ap_.tensor, ap_.offset, a)

def emit_attn_p1(p, g, upto="full"):
    nc = p.nc
    MQ, WT, SCR, PS = g['MQ'], g['WT'], g['SCR'], g['PS']
    SCRF = g['SCRF']

    LN1F = g['LN1F']
    p.stage()
    def zf(e):
        for i in range(NCT):
            ii = e.memset(LN1F[i][:, 0:1], 0.0)
        p.inc(ii, "dve", 1)
    p.on("vector", zf)
    p.stage()
    def ldln(e):
        lnf = g['lnbuf'].ap()
        for i in range(NCT):
            dma(p, e, "dS", LN1F[i][:, 1:2049], lnf[i*128:(i+1)*128, :])
    p.on("sync", ldln)

    def gemm_pass(nk, mw, lhsT_of, rhs_prep, rhs_of, evac_of, evac_eng="vector", spill_of=None,
                  mtbatch=4):
        nmt = (mw + 127) // 128
        for q in range(4):
            if rhs_prep is not None:
                p.stage(); p.on("vector", (lambda q: lambda e: rhs_prep(e, q))(q))
            for mh in range(0, nmt, mtbatch):
                mts = list(range(mh, min(mh+mtbatch, nmt)))
                p.stage()
                def mmst(e, mts=mts, q=q):
                    for ii_, mt in enumerate(mts):
                        nm = min(128, mw - mt*128)
                        for k in range(nk):
                            mm(p, e, PS[ii_][0:nm, :], lhsT_of(k, mt, nm), rhs_of(k, q),
                               start=(k==0), stop=(k==nk-1), inc=(k==nk-1))
                p.on("tensor", mmst)
                p.stage()
                def ev(e, mts=mts, q=q):
                    for ii_, mt in enumerate(mts):
                        evac_of(e, mt, q, PS[ii_], ii_)
                p.on(evac_eng, ev)
                if spill_of is not None:
                    p.stage()
                    def sp(e, mts=mts, q=q):
                        for ii_, mt in enumerate(mts):
                            spill_of(e, mt, q, ii_)
                    p.on("sync", sp)

    def mix_prep(vi):
        def prep(e, q):
            for k in range(NCT):
                ii = e.tensor_tensor(out=MQ[k][:, :], in0=LN1F[k][:, q*512:q*512+512],
                                     in1=LN1F[k][:, 1+q*512:1+q*512+512], op=OP.subtract)
            p.inc(ii, "dve", 1); p.ser(e, "dve")
            for k in range(NCT):
                ii = e.scalar_tensor_tensor(out=MQ[k][:, :], in0=MQ[k][:, :],
                                            scalar=g['VEC'][k][:, vi:vi+1],
                                            in1=LN1F[k][:, 1+q*512:1+(q+1)*512],
                                            op0=OP.mult, op1=OP.add)
            p.inc(ii, "dve", 1)
        return prep

    def loadW(wname, mw, dst=None, nk=NCT, rows=128):
        dst = dst or WT
        p.stage()
        def ld(e, w=g[wname]):
            for k in range(nk):
                dma(p, e, "dG", dst[k][0:rows, 0:mw], w[k*rows:(k+1)*rows, 0:mw])
        p.on("gpsimd", ld)

    def ev_cast_scr(e, mt, q, ps, slot, rows=128):
        ii = e.tensor_copy(out=SCR[slot][0:rows, :], in_=ps[0:rows, :])
        p.inc(ii, "dve", 1)

    def sp_scr(sdram, mw):
        def sp(e, mt, q, slot):
            nm = min(128, mw - mt*128)
            dma(p, e, "dS", sdram[mt*128:mt*128+nm, q*512:(q+1)*512], SCR[slot][0:nm, :])
        return sp

    vi = {'r':0,'w':1,'k':2,'v':3,'a':4,'g':5}

    SGT = g['SGT']
    loadW('g1', 128)
    gemm_pass(NCT, 128, lambda k, mt, nm: WT[k][:, 0:128], mix_prep(vi['g']),
              lambda k, q: MQ[k][:, :],
              lambda e, mt, q, ps, slot: p.inc(e.activation(out=SGT[:, q*512:(q+1)*512], in_=ps[:, :], func=AF.Sigmoid), "act", 1),
              evac_eng="scalar")
    loadW('g2_h', 2048, dst=g['WT2'], nk=1)
    gemm_pass(1, 2048, lambda k, mt, nm: g['WT2'][0][:, mt*128:mt*128+nm], None,
              lambda k, q: SGT[:, q*512:(q+1)*512],
              ev_cast_scr, spill_of=sp_scr(g['sG'], 2048))

    TW = g['TW']
    loadW('w1', 64)
    gemm_pass(NCT, 64, lambda k, mt, nm: WT[k][:, 0:64], mix_prep(vi['w']),
              lambda k, q: MQ[k][:, :],
              lambda e, mt, q, ps, slot: p.inc(e.activation(out=TW[:, q*512:(q+1)*512], in_=ps[0:64, :], func=AF.Tanh), "act", 1),
              evac_eng="scalar")
    loadW('w2_h', 2048, dst=g['WT2'], nk=1, rows=64)
    def ev_ldp(e, mt, q, ps, slot):
        i1 = e.activation(out=SCRF[0][:, :], in_=ps[:, :], func=AF.Exp, scale=-1.0, bias=g['negw0'][:, mt:mt+1])
        p.inc(i1, "act", 1); p.ser(e, "act")
        i2 = e.activation(out=SCRF[1][:, :], in_=SCRF[0][:, :], func=AF.Ln, bias=g['c_one'][:, 0:1])
        p.inc(i2, "act", 1); p.ser(e, "act")
        ii = e.activation(out=SCRF[0][:, :], in_=SCRF[1][:, :], func=AF.Exp, scale=-1.0, bias=g['c_neghalf'][:, 0:1])
        p.inc(ii, "act", 1)
    def sp_ldp(e, mt, q, slot):
        dma(p, e, "dS", g['sLDP'][mt*128:(mt+1)*128, q*512:(q+1)*512], SCRF[0][:, :])
    gemm_pass(1, 2048, lambda k, mt, nm: g['WT2'][0][0:64, mt*128:mt*128+nm], None,
              lambda k, q: TW[0:64, q*512:(q+1)*512],
              ev_ldp, evac_eng="scalar", spill_of=sp_ldp, mtbatch=1)

    loadW('a1', 64)
    gemm_pass(NCT, 64, lambda k, mt, nm: WT[k][:, 0:64], mix_prep(vi['a']),
              lambda k, q: MQ[k][:, :],
              lambda e, mt, q, ps, slot: p.inc(e.tensor_copy(out=TW[:, q*512:(q+1)*512], in_=ps[0:64, :]), "dve", 1))
    loadW('a2_h', 2048, dst=g['WT2'], nk=1, rows=64)
    def ev_ag(e, mt, q, ps, slot):
        ii = e.activation(out=SCR[slot][:, :], in_=ps[:, :], func=AF.Sigmoid, bias=g['VECH'][mt][:, 1:2])
        p.inc(ii, "act", 1)
    gemm_pass(1, 2048, lambda k, mt, nm: g['WT2'][0][0:64, mt*128:mt*128+nm], None,
              lambda k, q: TW[0:64, q*512:(q+1)*512],
              ev_ag, evac_eng="scalar", spill_of=sp_scr(g['sAG'], 2048))

    for nm_, wname, sd in (('v', 'W_v_h', 'sV'), ('r', 'W_r_h', 'sR'), ('k', 'W_k_h', 'sKraw')):
        loadW(wname, 2048)
        gemm_pass(NCT, 2048, lambda k, mt, nm: WT[k][:, mt*128:mt*128+nm], mix_prep(vi[nm_]),
                  lambda k, q: MQ[k][:, :], ev_cast_scr, spill_of=sp_scr(g[sd], 2048))
    if upto == "proj":
        return


def emit_attn_p2(p, g):
    nc = p.nc
    MQ, WT, SCR, PS = g['MQ'], g['WT'], g['SCR'], g['PS']
    SCRF = g['SCRF']
    KRW, AGB, K2B, SQT = g['KRW'], g['AGB'], g['K2B'], g['SQT']
    RN2, RR2 = g['RN2'], g['RR2']
    VH = g['VECH']
    for m in range(HCT):
        p.stage()
        def ld(e, m=m):
            dma(p, e, "dS", KRW[:, :], g['sKraw'][m*128:(m+1)*128, :])
            dma(p, e, "dS", AGB[:, :], g['sAG'][m*128:(m+1)*128, :])
        p.on("sync", ld)
        p.stage()
        def kkr(e, m=m):
            ii = e.tensor_scalar(out=K2B[:, :], in0=KRW[:, :], scalar1=VH[m][:, 2:3], scalar2=None, op0=OP.mult)
            p.inc(ii, "dve", 1)
        p.on("vector", kkr)
        p.stage()
        def sq(e):
            ii = e.activation(out=SQT[:, :], in_=K2B[:, :], func=AF.Square)
            p.inc(ii, "act", 1)
        p.on("scalar", sq)
        p.stage()
        def ssum(e):
            for q in range(4):
                mm(p, e, PS[q][0:2, :], g['on2'][:, :], SQT[:, q*512:(q+1)*512])
        p.on("tensor", ssum)
        p.stage()
        def srt(e):
            for q in range(4):
                ii = e.activation(out=RN2[0:2, q*512:(q+1)*512], in_=PS[q][0:2, :], func=AF.Sqrt)
            p.inc(ii, "act", 1)
        p.on("scalar", srt)
        p.stage()
        def rec(e):
            i1 = e.tensor_scalar(out=RN2[:, :], in0=RN2[:, :], scalar1=1e-12, scalar2=None, op0=OP.max)
            p.inc(i1, "dve", 1); p.ser(e, "dve")
            ii = e.reciprocal(out=RR2[:, :], in_=RN2[:, :])
            p.inc(ii, "dve", 1)
        p.on("vector", rec)
        p.stage()
        def bc(e):
            for q in range(4):
                mm(p, e, PS[q][:, :], g['sel2f'][:, :], RR2[:, q*512:(q+1)*512])
        p.on("tensor", bc)
        p.stage()
        def kkn(e):
            for q in range(4):
                ii = e.tensor_tensor(out=SQT[:, q*512:(q+1)*512], in0=K2B[:, q*512:(q+1)*512],
                                     in1=PS[q][:, :], op=OP.mult)
            p.inc(ii, "dve", 1)
        p.on("vector", kkn)
        p.stage()
        def spkk(e, m=m):
            dma(p, e, "dS", g['sKK'][m*128:(m+1)*128, :], SQT[:, :])
        p.on("sync", spkk)
        p.stage()
        def k2b(e, m=m):
            i1 = e.tensor_scalar(out=K2B[:, :], in0=AGB[:, :], scalar1=VH[m][:, 3:4],
                                 scalar2=g['mka'][:, m:m+1], op0=OP.mult, op1=OP.add)
            p.inc(i1, "dve", 1); p.ser(e, "dve")
            i2 = e.tensor_tensor(out=K2B[:, :], in0=KRW[:, :], in1=K2B[:, :], op=OP.mult)
            p.inc(i2, "dve", 1); p.ser(e, "dve")
            ii = e.tensor_tensor(out=KRW[:, :], in0=SQT[:, :], in1=AGB[:, :], op=OP.mult)
            p.inc(ii, "dve", 1)
        p.on("vector", k2b)
        p.stage()
        def spk2(e, m=m):
            dma(p, e, "dS", g['sK2'][m*128:(m+1)*128, :], K2B[:, :])
            dma(p, e, "dS", g['sB_'][m*128:(m+1)*128, :], KRW[:, :])
            dma(p, e, "dS", AGB[:, :], g['sR'][m*128:(m+1)*128, :])
            dma(p, e, "dS", SQT[:, :], g['sV'][m*128:(m+1)*128, :])
        p.on("sync", spk2)
        p.stage()
        def rb(e, m=m):
            i1 = e.tensor_tensor(out=KRW[:, :], in0=AGB[:, :], in1=K2B[:, :], op=OP.mult)
            p.inc(i1, "dve", 1); p.ser(e, "dve")
            ii = e.tensor_scalar(out=KRW[:, :], in0=KRW[:, :], scalar1=VH[m][:, 6:7], scalar2=None, op0=OP.mult)
            p.inc(ii, "dve", 1)
        p.on("vector", rb)
        p.stage()
        def bsum(e):
            for q in range(4):
                mm(p, e, PS[q][0:2, :], g['on2'][:, :], KRW[:, q*512:(q+1)*512])
        p.on("tensor", bsum)
        p.stage()
        def bev(e):
            for q in range(4):
                ii = e.tensor_copy(out=RR2[0:2, q*512:(q+1)*512], in_=PS[q][0:2, :])
            p.inc(ii, "dve", 1)
        p.on("vector", bev)
        p.stage()
        def bbc(e):
            for q in range(4):
                mm(p, e, PS[q][:, :], g['sel2f'][:, :], RR2[:, q*512:(q+1)*512])
        p.on("tensor", bbc)
        p.stage()
        def bon(e):
            for q in range(4):
                ii = e.tensor_tensor(out=KRW[:, q*512:(q+1)*512], in0=SQT[:, q*512:(q+1)*512],
                                     in1=PS[q][:, :], op=OP.mult)
            p.inc(ii, "dve", 1)
        p.on("vector", bon)
        p.stage()
        def spbon(e, m=m):
            dma(p, e, "dS", g['sOACC'][m*128:(m+1)*128, :], KRW[:, :])
        p.on("sync", spbon)


def emit_attn_p3(p, g):
    nc = p.nc
    MQ, WT, SCR, PS = g['MQ'], g['WT'], g['SCR'], g['PS']
    SCRF = g['SCRF']
    VH = g['VECH']
    RHB, KHB, BHB, ANB, VB = g['RHB'], g['KHB'], g['BHB'], g['ANB'], g['VB']
    EDB, EIB, TMB = g['EDB'], g['EIB'], g['TMB']
    LDPF, CLDF = g['LDPF'], g['CLDF']
    DLC = g['DLC']
    VTt, KTTt, BTTt = g['VTt'], g['KTTt'], g['BTTt']
    MAKw, MRBw, MRKw = g['MAKw'], g['MRBw'], g['MRKw']
    HC = g['HC']
    GTw, RATw, YVw, Uw = g['GTw'], g['RATw'], g['YVw'], g['Uw']
    DIAG = g['DIAG']
    PST = g['PST']
    onesbf, identbf = g['onesbf'], g['identbf']
    NJm = 16

    def fr3(ap_, nblk, blklen):
        a = list(ap_.ap)
        assert a[-1][1] == nblk*blklen
        a = a[:-1] + [[blklen, nblk], [a[-1][0], blklen]]
        return AP(ap_.tensor, ap_.offset, a)

    for m in range(HCT):
        UPc, LOc, UPn, LOn = g['UPc'], g['LOc'], g['UPn'], g['LOn']
        p.stage()
        def ld(e, m=m):
            dma(p, e, "dS", RHB[:, :], g['sR'][m*128:(m+1)*128, :])
            dma(p, e, "dS", KHB[:, :], g['sK2'][m*128:(m+1)*128, :])
            dma(p, e, "dS", BHB[:, :], g['sB_'][m*128:(m+1)*128, :])
            dma(p, e, "dS", ANB[:, :], g['sKK'][m*128:(m+1)*128, :])
            dma(p, e, "dS", VB[:, :], g['sV'][m*128:(m+1)*128, :])
            dma(p, e, "dS", LDPF[:, :], g['sLDP'][m*128:(m+1)*128, :])
        p.on("sync", ld)
        p.stage()
        def cum(e):
            for j in range(NJm):
                ii = e.tensor_tensor_scan(out=CLDF[:, j*128:(j+1)*128], data0=onesbf[:, :],
                                          data1=LDPF[:, j*128:(j+1)*128], initial=0.0,
                                          op0=OP.mult, op1=OP.add)
            p.inc(ii, "dve", 1)
        p.on("vector", cum)
        p.stage()
        def exps(e):
            e.activation(out=EDB[:, :], in_=CLDF[:, :], func=AF.Exp, scale=-1.0)
            e.activation(out=EIB[:, :], in_=CLDF[:, :], func=AF.Exp)
            ii = e.activation(out=TMB[:, :], in_=LDPF[:, :], func=AF.Exp)
            p.inc(ii, "act", 1)
        p.on("scalar", exps)
        p.stage()
        def derv(e):
            i1 = e.tensor_tensor(out=TMB[:, :], in0=EDB[:, :], in1=TMB[:, :], op=OP.mult)
            p.inc(i1, "dve", 1); p.ser(e, "dve")
            i2 = e.tensor_scalar(out=TMB[:, :], in0=TMB[:, :], scalar1=-1.0, scalar2=None, op0=OP.mult)
            p.inc(i2, "dve", 1); p.ser(e, "dve")
            e.tensor_tensor(out=ANB[:, :], in0=ANB[:, :], in1=TMB[:, :], op=OP.mult)
            e.tensor_tensor(out=RHB[:, :], in0=RHB[:, :], in1=EDB[:, :], op=OP.mult)
            e.tensor_tensor(out=KHB[:, :], in0=KHB[:, :], in1=EIB[:, :], op=OP.mult)
            e.tensor_tensor(out=BHB[:, :], in0=BHB[:, :], in1=EIB[:, :], op=OP.mult)
            ii = e.tensor_copy(out=DLC[:, :], in_=AP(EDB.ap().tensor, EDB.ap().offset + 127,
                                                     [list(EDB.ap().ap[0]), [128, NJm]]))
            p.inc(ii, "dve", 1)
        p.on("vector", derv)
        for sname, srcbuf, dst in (("v", VB, VTt), ("kt", KHB, KTTt), ("bt", BHB, BTTt)):
            if sname != "v":
                p.stage()
                def mkt(e, srcbuf=srcbuf):
                    ii = e.tensor_tensor(out=fr3(TMB[:, :], NJm, 128), in0=fr3(srcbuf[:, :], NJm, 128),
                                         in1=bcast_ap(DLC[:, :], 2, 128), op=OP.mult)
                    p.inc(ii, "dve", 1)
                p.on("vector", mkt)
                src = TMB
            else:
                src = VB
            for h4 in range(0, NJm, 4):
                p.stage()
                def tps(e, src=src, h4=h4):
                    for s in range(4):
                        tp128(p, e, PST, identbf, src[:, (h4+s)*128:(h4+s+1)*128], s)
                p.on("tensor", tps)
                p.stage()
                def tpe(e, dst=dst, h4=h4):
                    ii = e.tensor_copy(out=dst[:, h4*128:(h4+4)*128], in_=PST[:, 0:512])
                    p.inc(ii, "dve", 1)
                p.on("vector", tpe)
        kinds = [(BHB, ANB, 'mSU', UPc), (ANB, BHB, 'mSL', LOc), (KHB, ANB, 'mSU', MAKw),
                 (BHB, RHB, 'mUI', MRBw), (KHB, RHB, 'mUI', MRKw)]
        for lhs, rhs, mk, dst in kinds:
            def emm(e, u, b, lhs=lhs, rhs=rhs):
                j, ha = u // 2, u % 2
                sl = slice(ha*64, ha*64+64)
                mm(p, e, PS[b][:, 0:128], lhs[sl, j*128:(j+1)*128], rhs[sl, j*128:(j+1)*128])
            def eev(e, u, b, mk=mk, dst=dst):
                ii = e.tensor_tensor(out=dst[:, u*128:(u+1)*128], in0=PS[b][:, 0:128],
                                     in1=g[mk][:, :], op=OP.mult)
                p.inc(ii, "dve", 1)
            wavepipe(p, 32, emm, eev)
        for h4 in range(0, NJm, 4):
            p.stage()
            def tpa(e, h4=h4):
                for s in range(4):
                    tp128(p, e, PST, identbf, ANB[:, (h4+s)*128:(h4+s+1)*128], s)
            p.on("tensor", tpa)
            p.stage()
            def tpae(e, h4=h4):
                for s in range(4):
                    j = h4+s
                    e.tensor_copy(out=HC[:, j*256:j*256+64], in_=PST[:, s*128:s*128+64])
                    ii = e.tensor_copy(out=HC[:, j*256+128:j*256+192], in_=PST[:, s*128+64:s*128+128])
                p.inc(ii, "dve", 1)
            p.on("vector", tpae)
        def etv(e, u, b):
            j, ha = u // 2, u % 2
            mm(p, e, PS[b][:, 0:64], MAKw[:, u*128:(u+1)*128], VTt[:, j*128+ha*64 : j*128+ha*64+64])
        def etve(e, u, b):
            j, ha = u // 2, u % 2
            ii = e.tensor_copy(out=HC[:, j*256+ha*128+64 : j*256+ha*128+128], in_=PS[b][0:128, 0:64])
            p.inc(ii, "dve", 1)
        wavepipe(p, 32, etv, etve)
        lev = 1
        while lev <= 64:
            def eap(e, u, b, UPc=UPc):
                j, ha = u // 2, u % 2
                mm(p, e, PS[b][:, 0:128], UPc[:, u*128:(u+1)*128],
                   HC[:, j*256+ha*128 : j*256+(ha+1)*128])
            def eape(e, u, b):
                j, ha = u // 2, u % 2
                hv = HC[:, j*256+ha*128 : j*256+(ha+1)*128]
                ii = e.tensor_tensor(out=hv, in0=PS[b][:, 0:128], in1=hv, op=OP.add)
                p.inc(ii, "dve", 1)
            wavepipe(p, 32, eap, eape)
            if lev < 64:
                for (A, B_, D) in ((LOc, UPc, UPn), (UPc, LOc, LOn)):
                    if lev == 32 and D is LOn:
                        continue
                    def esq(e, u, b, A=A, B_=B_):
                        mm(p, e, PS[b][:, 0:128], A[:, u*128:(u+1)*128], B_[:, u*128:(u+1)*128])
                    def esqe(e, u, b, D=D):
                        ii = e.tensor_copy(out=D[:, u*128:(u+1)*128], in_=PS[b][:, 0:128])
                        p.inc(ii, "dve", 1)
                    wavepipe(p, 32, esq, esqe)
                g['UPc'], g['UPn'] = g['UPn'], g['UPc']
                g['LOc'], g['LOn'] = g['LOn'], g['LOc']
                UPc, LOc, UPn, LOn = g['UPc'], g['LOc'], g['UPn'], g['LOn']
            lev *= 2
        p.stage()
        def gtd(e):
            for u in range(32):
                j, ha = u // 2, u % 2
                ii = e.tensor_scalar(out=DIAG[0:64, u*64:(u+1)*64], in0=identbf[0:64, 0:64],
                                     scalar1=DLC[ha*64:ha*64+64, j:j+1], scalar2=None, op0=OP.mult)
            p.inc(ii, "dve", 1)
        p.on("vector", gtd)
        def egt(e, u, b):
            j, ha = u // 2, u % 2
            mm(p, e, PS[b][0:64, 0:64], HC[:, j*256+ha*128 : j*256+ha*128+64],
               BTTt[:, j*128+ha*64 : j*128+ha*64+64])
        def egte(e, u, b):
            ii = e.tensor_tensor(out=GTw[0:64, u*64:(u+1)*64], in0=PS[b][0:64, 0:64],
                                 in1=DIAG[0:64, u*64:(u+1)*64], op=OP.add)
            p.inc(ii, "dve", 1)
        wavepipe(p, 32, egt, egte)
        def erat(e, u, b):
            j, ha = u // 2, u % 2
            sl = slice(ha*64, ha*64+64)
            pv = PS[b][0:64, 0:128]
            mm(p, e, pv, HC[:, j*256+ha*128 : j*256+ha*128+64], MRBw[:, u*128:(u+1)*128],
               start=True, stop=False, inc=False)
            mm(p, e, pv, identbf[sl, sl], RHB[sl, j*128:(j+1)*128], start=False, stop=True)
        def erate(e, u, b):
            ii = e.tensor_copy(out=RATw[0:64, u*128:(u+1)*128], in_=PS[b][0:64, 0:128])
            p.inc(ii, "dve", 1)
        wavepipe(p, 32, erat, erate)
        def eyv(e, u, b):
            j, ha = u // 2, u % 2
            pv = PS[b][:, 0:64]
            mm(p, e, pv, MRBw[:, u*128:(u+1)*128], HC[:, j*256+ha*128+64 : j*256+ha*128+128],
               start=True, stop=False, inc=False)
            mm(p, e, pv, MRKw[:, u*128:(u+1)*128], VTt[:, j*128+ha*64 : j*128+ha*64+64],
               start=False, stop=True)
        def eyve(e, u, b):
            ii = e.tensor_copy(out=YVw[:, u*64:(u+1)*64], in_=PS[b][:, 0:64])
            p.inc(ii, "dve", 1)
        wavepipe(p, 32, eyv, eyve)
        def eu(e, u, b):
            j, ha = u // 2, u % 2
            pv = PS[b][0:64, 0:64]
            mm(p, e, pv, BTTt[:, j*128+ha*64 : j*128+ha*64+64],
               HC[:, j*256+ha*128+64 : j*256+ha*128+128], start=True, stop=False, inc=False)
            mm(p, e, pv, KTTt[:, j*128+ha*64 : j*128+ha*64+64],
               VTt[:, j*128+ha*64 : j*128+ha*64+64], start=False, stop=True)
        def eue(e, u, b):
            ii = e.tensor_copy(out=Uw[0:64, u*64:(u+1)*64], in_=PS[b][0:64, 0:64])
            p.inc(ii, "dve", 1)
        wavepipe(p, 32, eu, eue)
        p.stage()
        def sprec(e, m=m):
            dma(p, e, "dS", g['sGT'][m*64:(m+1)*64, :], GTw[0:64, :])
            dma(p, e, "dS", g['sRAT'][m*64:(m+1)*64, :], RATw[0:64, :])
            dma(p, e, "dS", g['sYV'][m*128:(m+1)*128, :], YVw[:, :])
            dma(p, e, "dS", g['sU'][m*64:(m+1)*64, :], Uw[0:64, :])
        p.on("sync", sprec)


def emit_attn_p4(p, g, upto="full"):
    nc = p.nc
    MQ, WT, SCR, PS = g['MQ'], g['WT'], g['SCR'], g['PS']
    SCRF = g['SCRF']
    VH = g['VECH']
    PST = g['PST']
    identbf = g['identbf']
    ZA, ZB = g['ZA'], g['ZB']
    GTld, RATld, YVld, Uld = g['GTld'], g['RATld'], g['YVld'], g['Uld']
    YSB = g['YSB']
    OACCB = g['OACCB']
    TPGN = g['TPGN']
    p.stage()
    def zinit(e):
        e.memset(ZA[:, :], 0.0)
        for m in range(HCT):
            pass
        ii = e.memset(ZB[:, :], 0.0)
        p.inc(ii, "dve", 1)
    p.on("vector", zinit)
    p.stage()
    def oacl(e):
        for m in range(HCT):
            dma(p, e, "dS", OACCB[m][:, :], g['sOACC'][m*128:(m+1)*128, :])
    p.on("sync", oacl)
    for j in range(16):
        Zc = ZA if j % 2 == 0 else ZB
        Zn = ZB if j % 2 == 0 else ZA
        p.stage()
        def recl(e, j=j):
            for m in range(HCT):
                dma(p, e, "dS", GTld[0:64, m*128:(m+1)*128], g['sGT'][m*64:(m+1)*64, 2*j*64:(2*j+2)*64])
                dma(p, e, "dS", RATld[0:64, m*256:(m+1)*256], g['sRAT'][m*64:(m+1)*64, 2*j*128:(2*j+2)*128])
                dma(p, e, "dS", YVld[:, m*128:(m+1)*128], g['sYV'][m*128:(m+1)*128, 2*j*64:(2*j+2)*64])
                dma(p, e, "dS", Uld[0:64, m*128:(m+1)*128], g['sU'][m*64:(m+1)*64, 2*j*64:(2*j+2)*64])
        p.on("sync", recl)
        def ech(e, u, b, j=j, Zc=Zc):
            m, ha, t = u // 4, (u // 2) % 2, u % 2
            if t == 0:
                pv = PS[b][:, 0:64]
                mm(p, e, pv, RATld[0:64, m*256+ha*128:m*256+(ha+1)*128],
                   Zc[0:64, m*128+ha*64:m*128+ha*64+64], start=True, stop=False, inc=False)
                mm(p, e, pv, identbf[:, :], YVld[:, m*128+ha*64:m*128+ha*64+64],
                   start=False, stop=True)
            else:
                pv = PS[b][0:64, 0:64]
                mm(p, e, pv, GTld[0:64, m*128+ha*64:m*128+ha*64+64],
                   Zc[0:64, m*128+ha*64:m*128+ha*64+64], start=True, stop=False, inc=False)
                mm(p, e, pv, identbf[0:64, 0:64], Uld[0:64, m*128+ha*64:m*128+ha*64+64],
                   start=False, stop=True)
        def eche(e, u, b, j=j, Zn=Zn):
            m, ha, t = u // 4, (u // 2) % 2, u % 2
            if t == 0:
                ii = e.tensor_copy(out=YSB[:, (m*2+ha)*64:(m*2+ha+1)*64], in_=PS[b][:, 0:64])
            else:
                ii = e.tensor_copy(out=Zn[0:64, m*128+ha*64:m*128+ha*64+64], in_=PS[b][0:64, 0:64])
            p.inc(ii, "dve", 1)
        wavepipe(p, 64, ech, eche, nbanks=6)
        for mh in range(0, HCT, 4):
            p.stage()
            def ytp(e, mh=mh):
                for s in range(4):
                    tp128(p, e, PST, identbf, YSB[:, (mh+s)*128:(mh+s+1)*128], s)
            p.on("tensor", ytp)
            p.stage()
            def ygn(e, mh=mh, j=j):
                for s in range(4):
                    m = mh + s
                    ii = e.tensor_scalar(out=TPGN[:, s*128:(s+1)*128], in0=PST[:, s*128:(s+1)*128],
                                         scalar1=VH[m][:, 4:5], scalar2=VH[m][:, 5:6],
                                         op0=OP.mult, op1=OP.add)
                p.inc(ii, "dve", 1); p.ser(e, "dve")
                for s in range(4):
                    m = mh + s
                    ii = e.tensor_tensor(out=OACCB[m][:, j*128:(j+1)*128],
                                         in0=OACCB[m][:, j*128:(j+1)*128], in1=TPGN[:, s*128:(s+1)*128], op=OP.add)
                p.inc(ii, "dve", 1)
            p.on("vector", ygn)
    if upto == "oacc":
        return
    p.stage()
    def gld(e):
        for m in range(HCT):
            dma(p, e, "dS", g['WG'][m][:, :], g['sG'][m*128:(m+1)*128, :])
    p.on("sync", gld)
    p.stage()
    def ogm(e):
        for m in range(HCT):
            ii = e.tensor_tensor(out=OACCB[m][:, :], in0=OACCB[m][:, :],
                                 in1=g['WG'][m][:, :], op=OP.mult)
        p.inc(ii, "dve", 1)
    p.on("vector", ogm)
    p.stage()
    def wol(e):
        for k in range(HCT):
            dma(p, e, "dG", g['WOT'][k][:, :], g['W_o_h'][k*128:(k+1)*128, :])
    p.on("gpsimd", wol)
    for q in range(4):
        for mh in range(0, 16, 4):
            p.stage()
            def wmm(e, q=q, mh=mh):
                for s in range(4):
                    mo = mh + s
                    for k in range(HCT):
                        mm(p, e, PS[s][:, :], g['WOT'][k][:, mo*128:(mo+1)*128],
                           OACCB[k][:, q*512:(q+1)*512], start=(k==0), stop=(k==HCT-1), inc=(k==HCT-1))
            p.on("tensor", wmm)
            p.stage()
            def wev(e, q=q, mh=mh):
                for s in range(0, 2):
                    ii = e.tensor_copy(out=SCRF[s][:, :], in_=PS[s][:, :])
                p.inc(ii, "dve", 1)
            p.on("vector", wev)
            p.stage()
            def wsp(e, q=q, mh=mh):
                for s in range(0, 2):
                    dma(p, e, "dS", g['attnsum'][(mh+s)*128:(mh+s+1)*128, q*512:(q+1)*512], SCRF[s][:, :])
            p.on("sync", wsp)
            p.stage()
            def wev2(e, q=q, mh=mh):
                for s in range(2, 4):
                    ii = e.tensor_copy(out=SCRF[s-2][:, :], in_=PS[s][:, :])
                p.inc(ii, "dve", 1)
            p.on("vector", wev2)
            p.stage()
            def wsp2(e, q=q, mh=mh):
                for s in range(2, 4):
                    dma(p, e, "dS", g['attnsum'][(mh+s)*128:(mh+s+1)*128, q*512:(q+1)*512], SCRF[s-2][:, :])
            p.on("sync", wsp2)

"""FFN phase (dense 8-expert MoE + shared). Processes tokens in 2 halves of 1024."""

AXX = mybir.AxisListType.X

def emit_ffn(p, g, lnsrc, outsum):
    nc = p.nc
    PS = g['PS']
    LNT, WF, HP, ROUT = g['LNT'], g['WF'], g['HP'], g['ROUT']
    SCORES, COMBR, SGR, SGB, CMB = g['SCORES'], g['COMBR'], g['SGR'], g['SGB'], g['CMB']
    SCR = g['SCR']
    RT = g['RT']

    def gpass(nk, wof, rhs_of, nmt, evac, kmax=None):
        for q in range(2):
            for mh in range(0, nmt, 4):
                p.stage()
                def mmst(e, q=q, mh=mh):
                    for s in range(4):
                        mt = mh + s
                        for k in range(nk):
                            mm(p, e, PS[s][:, :], wof(k, mt), rhs_of(k, q),
                               start=(k==0), stop=(k==nk-1), inc=(k==nk-1))
                p.on("tensor", mmst)
                p.stage()
                def ev(e, q=q, mh=mh):
                    evac(e, q, mh)
                p.on(g.get('_evac_eng', 'vector'), ev)

    for th in range(2):
        tof = th*1024
        p.stage()
        def ldln(e, tof=tof):
            for i in range(NCT):
                dma(p, e, "dS", LNT[i][:, :], lnsrc[i*128:(i+1)*128, tof:tof+1024])
        p.on("sync", ldln)
        p.stage()
        def ldgw(e):
            for k in range(NCT):
                dma(p, e, "dG", g['GWB'][k][:, 0:8], g['gate_wT'][k*128:(k+1)*128, :])
                dma(p, e, "dG", g['GWB'][k][:, 8:9], g['se_gT'][k*128:(k+1)*128, :])
        p.on("gpsimd", ldgw)
        for q in range(2):
            p.stage()
            def smm(e, q=q):
                for k in range(NCT):
                    mm(p, e, PS[0][0:8, :], g['GWB'][k][:, 0:8], LNT[k][:, q*512:(q+1)*512],
                       start=(k==0), stop=(k==NCT-1), inc=(k==NCT-1))
                for k in range(NCT):
                    mm(p, e, PS[1][0:1, :], g['GWB'][k][:, 8:9], LNT[k][:, q*512:(q+1)*512],
                       start=(k==0), stop=(k==NCT-1), inc=(k==NCT-1))
            p.on("tensor", smm)
            p.stage()
            def sev(e, q=q):
                ii = e.tensor_copy(out=SCORES[0:8, q*512:(q+1)*512], in_=PS[0][0:8, :])
                p.inc(ii, "dve", 1)
            p.on("vector", sev)
            def gev(e, q=q):
                ii = e.activation(out=SGR[0:1, q*512:(q+1)*512], in_=PS[1][0:1, :], func=AF.Sigmoid)
                p.inc(ii, "act", 1)
            p.on("scalar", gev)
        for q in range(2):
            p.stage()
            def gb(e, q=q):
                mm(p, e, PS[2][:, :], g['ones1_128'][0:1, :], SGR[0:1, q*512:(q+1)*512])
            p.on("tensor", gb)
            p.stage()
            def gbe(e, q=q):
                ii = e.tensor_copy(out=SGB[:, q*512:(q+1)*512], in_=PS[2][:, :])
                p.inc(ii, "dve", 1)
            p.on("vector", gbe)
        for i in range(8):
            p.stage()
            def tp1(e, i=i):
                j = e.transpose(PS[3][:, 0:8], SCORES[0:8, i*128:(i+1)*128], g['ident_f_sb'][0:8, 0:8])
                p.inc(j, "pe", 1)
            p.on("tensor", tp1)
            p.stage()
            def rt(e, i=i):
                i1 = e.tensor_copy(out=RT[:, 0:8], in_=PS[3][:, 0:8])
                p.inc(i1, "dve", 1); p.ser(e, "dve")
                i2 = e.tensor_reduce(out=g['RM1'][:, :], in_=RT[:, 0:8], axis=AXX, op=OP.max)
                p.inc(i2, "dve", 1); p.ser(e, "dve")
                i3 = e.tensor_scalar(out=g['MK1'][:, :], in0=RT[:, 0:8], scalar1=g['RM1'][:, 0:1],
                                     scalar2=None, op0=OP.is_equal)
                p.inc(i3, "dve", 1); p.ser(e, "dve")
                i4 = e.scalar_tensor_tensor(out=g['SC2'][:, :], in0=g['MK1'][:, :], scalar=-1e30,
                                            in1=RT[:, 0:8], op0=OP.mult, op1=OP.add)
                p.inc(i4, "dve", 1); p.ser(e, "dve")
                i5 = e.tensor_reduce(out=g['RM2'][:, :], in_=g['SC2'][:, :], axis=AXX, op=OP.max)
                p.inc(i5, "dve", 1); p.ser(e, "dve")
                i6 = e.tensor_scalar(out=g['MK2'][:, :], in0=g['SC2'][:, :], scalar1=g['RM2'][:, 0:1],
                                     scalar2=None, op0=OP.is_equal)
                p.inc(i6, "dve", 1); p.ser(e, "dve")
                i7 = e.tensor_tensor(out=g['DD'][:, :], in0=g['RM1'][:, :], in1=g['RM2'][:, :], op=OP.subtract)
                p.inc(i7, "dve", 1)
            p.on("vector", rt)
            p.stage()
            def sig(e):
                e.activation(out=g['W1'][:, :], in_=g['DD'][:, :], func=AF.Sigmoid)
                ii = e.activation(out=g['W2'][:, :], in_=g['DD'][:, :], func=AF.Sigmoid, scale=-1.0)
                p.inc(ii, "act", 1)
            p.on("scalar", sig)
            p.stage()
            def cmb(e):
                i1 = e.tensor_scalar(out=g['MK1'][:, :], in0=g['MK1'][:, :], scalar1=g['W1'][:, 0:1],
                                     scalar2=None, op0=OP.mult)
                i2 = e.tensor_scalar(out=g['MK2'][:, :], in0=g['MK2'][:, :], scalar1=g['W2'][:, 0:1],
                                     scalar2=None, op0=OP.mult)
                p.inc(i2, "dve", 1); p.ser(e, "dve")
                ii = e.tensor_tensor(out=RT[:, 0:8], in0=g['MK1'][:, :], in1=g['MK2'][:, :], op=OP.add)
                p.inc(ii, "dve", 1)
            p.on("vector", cmb)
            p.stage()
            def tp2(e, i=i):
                j = e.transpose(PS[3][0:8, 0:128], RT[:, 0:8], g['ident_f_sb'][:, :])
                p.inc(j, "pe", 1)
            p.on("tensor", tp2)
            p.stage()
            def cev(e, i=i):
                ii = e.tensor_copy(out=g['CB8'][0:8, i*128:(i+1)*128], in_=PS[3][0:8, 0:128])
                p.inc(ii, "dve", 1)
            p.on("vector", cev)
        p.stage()
        def cbd(e):
            dma(p, e, "dS", g['combd'].ap()[:, :], g['CB8'][0:8, :])
        p.on("sync", cbd)
        p.stage()
        def cbl(e):
            for ee in range(8):
                dma(p, e, "dS", COMBR[ee][0:1, :], g['combd'].ap()[ee:ee+1, :])
        p.on("sync", cbl)
        for ee in range(8):
            p.stage()
            def ldwg(e, ee=ee):
                for k in range(NCT):
                    dma(p, e, "dG", WF[k//2][:, (k%2)*1024:(k%2)*1024+1024],
                        g['wgT'][ee*2048+k*128 : ee*2048+(k+1)*128, :])
            p.on("gpsimd", ldwg)
            def hev(e, q, mh):
                for s in range(4):
                    ii = e.activation(out=HP[mh+s][:, q*512:(q+1)*512], in_=PS[s][:, :], func=AF.Silu)
                p.inc(ii, "act", 1)
            g['_evac_eng'] = 'scalar'
            gpass(NCT, lambda k, mt: WF[k//2][:, (k%2)*1024+mt*128:(k%2)*1024+(mt+1)*128],
                  lambda k, q: LNT[k][:, q*512:(q+1)*512], 8, hev)
            g['_evac_eng'] = 'vector'
            p.stage()
            def ldwu(e, ee=ee):
                for k in range(NCT):
                    dma(p, e, "dG", WF[k//2][:, (k%2)*1024:(k%2)*1024+1024],
                        g['wuT'][ee*2048+k*128 : ee*2048+(k+1)*128, :])
            p.on("gpsimd", ldwu)
            for q in range(2):
                p.stage()
                def cb(e, q=q, ee=ee):
                    mm(p, e, PS[4][:, :], g['ones1_128'][0:1, :], COMBR[ee][0:1, q*512:(q+1)*512])
                p.on("tensor", cb)
                p.stage()
                def cbe(e, q=q):
                    ii = e.tensor_copy(out=CMB[:, q*512:(q+1)*512], in_=PS[4][:, :])
                    p.inc(ii, "dve", 1)
                p.on("vector", cbe)
            def uev(e, q, mh):
                for s in range(4):
                    hv = HP[mh+s][:, q*512:(q+1)*512]
                    i1 = e.tensor_tensor(out=hv, in0=PS[s][:, :], in1=hv, op=OP.mult)
                    p.inc(i1, "dve", 1); p.ser(e, "dve")
                    ii = e.tensor_tensor(out=hv, in0=hv, in1=CMB[:, q*512:(q+1)*512], op=OP.mult)
                    p.inc(ii, "dve", 1)
            gpass(NCT, lambda k, mt: WF[k//2][:, (k%2)*1024+mt*128:(k%2)*1024+(mt+1)*128],
                  lambda k, q: LNT[k][:, q*512:(q+1)*512], 8, uev)
            p.stage()
            def ldwd(e, ee=ee):
                for k in range(8):
                    dma(p, e, "dG", WF[k][:, :], g['wdT'][ee*1024+k*128 : ee*1024+(k+1)*128, :])
            p.on("gpsimd", ldwd)
            def dev(e, q, mh, ee=ee):
                for s in range(4):
                    rv = ROUT[mh+s][:, q*512:(q+1)*512]
                    if ee == 0:
                        ii = e.tensor_copy(out=rv, in_=PS[s][:, :])
                    else:
                        ii = e.tensor_tensor(out=rv, in0=PS[s][:, :], in1=rv, op=OP.add)
                p.inc(ii, "dve", 1)
            gpass(8, lambda k, mt: WF[k][:, mt*128:(mt+1)*128],
                  lambda k, q: HP[k][:, q*512:(q+1)*512], 16, dev)
        for hh in range(2):
            p.stage()
            def ldsg(e, hh=hh):
                for k in range(NCT):
                    dma(p, e, "dG", WF[k//2][:, (k%2)*1024:(k%2)*1024+1024],
                        g['shgT'][k*128:(k+1)*128, hh*1024:(hh+1)*1024])
            p.on("gpsimd", ldsg)
            def shev(e, q, mh):
                for s in range(4):
                    ii = e.activation(out=HP[mh+s][:, q*512:(q+1)*512], in_=PS[s][:, :], func=AF.Silu)
                p.inc(ii, "act", 1)
            g['_evac_eng'] = 'scalar'
            gpass(NCT, lambda k, mt: WF[k//2][:, (k%2)*1024+mt*128:(k%2)*1024+(mt+1)*128],
                  lambda k, q: LNT[k][:, q*512:(q+1)*512], 8, shev)
            g['_evac_eng'] = 'vector'
            p.stage()
            def ldsu(e, hh=hh):
                for k in range(NCT):
                    dma(p, e, "dG", WF[k//2][:, (k%2)*1024:(k%2)*1024+1024],
                        g['shuT'][k*128:(k+1)*128, hh*1024:(hh+1)*1024])
            p.on("gpsimd", ldsu)
            def suev(e, q, mh):
                for s in range(4):
                    hv = HP[mh+s][:, q*512:(q+1)*512]
                    ii = e.tensor_tensor(out=hv, in0=PS[s][:, :], in1=hv, op=OP.mult)
                p.inc(ii, "dve", 1)
            gpass(NCT, lambda k, mt: WF[k//2][:, (k%2)*1024+mt*128:(k%2)*1024+(mt+1)*128],
                  lambda k, q: LNT[k][:, q*512:(q+1)*512], 8, suev)
            p.stage()
            def ldsd(e, hh=hh):
                for k in range(8):
                    dma(p, e, "dG", WF[k][:, :], g['shdT'][hh*1024+k*128 : hh*1024+(k+1)*128, :])
            p.on("gpsimd", ldsd)
            def sde(e, q, mh):
                for s in range(4):
                    rv = ROUT[mh+s][:, q*512:(q+1)*512]
                    i1 = e.tensor_tensor(out=SCR[s][:, :], in0=PS[s][:, :],
                                         in1=SGB[:, q*512:(q+1)*512], op=OP.mult)
                    p.inc(i1, "dve", 1); p.ser(e, "dve")
                    ii = e.tensor_tensor(out=rv, in0=rv, in1=SCR[s][:, :], op=OP.add)
                    p.inc(ii, "dve", 1)
            gpass(8, lambda k, mt: WF[k][:, mt*128:(mt+1)*128],
                  lambda k, q: HP[k][:, q*512:(q+1)*512], 16, sde)
        p.stage()
        def spo(e, tof=tof):
            for i in range(NCT):
                dma(p, e, "dS", outsum[i*128:(i+1)*128, tof:tof+1024], ROUT[i][:, :])
        p.on("sync", spo)

"""Mamba phase: full 4096 channels, T=2048, per-(ctile, n) TTS scan."""

TF = 2048
MCT = 32

def emit_mamba(p, g, lnsrc, outsum):
    nc = p.nc
    PS = g['PS']
    WM = g['WM']
    SCRF = g['SCRF']

    LN3T = g['LN3T']
    p.stage()
    def ldln(e):
        for i in range(NCT):
            dma(p, e, "dS", LN3T[i][:, :], lnsrc[i*128:(i+1)*128, :])
    p.on("sync", ldln)
    def spill_dst(mt):
        if mt < 32:   return g['zbuf'].ap(), mt*128
        if mt < 36:   return g['xbr'].ap(), (mt-32)*128
        if mt < 40:   return g['bbuf'].ap(), (mt-36)*128
        if mt < 72:   return g['cbuf'].ap(), (mt-40)*128
        return None, None
    for mt in range(73):
        p.stage()
        def ldw(e, mt=mt):
            for k in range(NCT):
                dma(p, e, "dG", WM[k][:, :], g['WinT'][k*128:(k+1)*128, mt*128:(mt+1)*128])
        p.on("gpsimd", ldw)
        for q in range(4):
            p.stage()
            def wmm(e, q=q):
                for k in range(NCT):
                    mm(p, e, PS[q % 2][:, :], WM[k][:, :], LN3T[k][:, q*512:(q+1)*512],
                       start=(k==0), stop=(k==NCT-1), inc=(k==NCT-1))
            p.on("tensor", wmm)
            p.stage()
            def wev(e, q=q, mt=mt):
                ii = e.tensor_copy(out=g['EV'][:, q*512:(q+1)*512], in_=PS[q % 2][:, :])
                p.inc(ii, "dve", 1)
            p.on("vector", wev)
        p.stage()
        def wsp(e, mt=mt):
            dstt, row = spill_dst(mt)
            if dstt is None:
                ii = e.dma_start(out=g['DTL'][:, :], in_=g['EV'][:, :])
                p.inc(ii, "dS", 16)
            else:
                dma(p, e, "dS", dstt[row:row+128, :], g['EV'][:, :])
        p.on("sync", wsp)
    p.stage()
    def ldvm(e):
        for m in range(MCT):
            dma(p, e, "dS", g['CW'][:, 4*m:4*m+4], g['conv_w'][m*128:(m+1)*128, :])
            dma(p, e, "dS", g['VM3'][:, 3*m:3*m+3], g['vecM'][m*128:(m+1)*128, :])
    p.on("sync", ldvm)
    p.stage()
    def mkbdt(e):
        for m in range(MCT):
            ii = e.tensor_scalar(out=g['BDT2'][:, m:m+1], in0=g['VM3'][:, 3*m:3*m+1],
                                 scalar1=2.0, scalar2=None, op0=OP.mult)
        p.inc(ii, "dve", 1)
    p.on("vector", mkbdt)
    p.stage()
    def lddt(e):
        for c2 in range(32):
            dma(p, e, "dG", WM[c2][:, :], g['WdtT'][:, c2*128:(c2+1)*128])
    p.on("gpsimd", lddt)
    for m in range(MCT):
        for q in range(4):
            p.stage()
            def dmm(e, m=m, q=q):
                mm(p, e, PS[q % 2][:, :], WM[m][:, :], g['DTL'][:, q*512:(q+1)*512])
            p.on("tensor", dmm)
            p.stage()
            def dev(e, m=m, q=q):
                i1 = e.activation(out=SCRF[0][:, :], in_=PS[q % 2][:, :], func=AF.Exp,
                                  bias=g['BDT2'][:, m:m+1])
                p.inc(i1, "act", 1); p.ser(e, "act")
                ii = e.activation(out=g['EV'][:, q*512:(q+1)*512], in_=SCRF[0][:, :], func=AF.Ln,
                                  bias=g['c_one'][:, 0:1])
                p.inc(ii, "act", 1)
            p.on("scalar", dev)
        p.stage()
        def dsp(e, m=m):
            dma(p, e, "dS", g['dbuf'].ap()[m*128:(m+1)*128, :], g['EV'][:, :])
        p.on("sync", dsp)
    XR, UB, DAB, SSB, CRP, YG = g['XR'], g['UB'], g['DAB'], g['SSB'], g['CRP'], g['YG']
    for m in range(MCT):
        p.stage()
        def ldxr(e, m=m):
            src = g['xbr'].ap()
            a = AP(src.tensor, src.offset + (16*m)*TF, [[0, 8], [TF, 16], [1, TF]])
            dma(p, e, "dS", XR[:, 3:3+TF], a)
            ii = e.dma_start(out=g['EV'][:, 0:2048], in_=g['dbuf'].ap()[m*128:(m+1)*128, :])
            p.inc(ii, "dS", 16)
        p.on("sync", ldxr)
        p.stage()
        def zpad(e):
            ii = e.memset(XR[:, 0:3], 0.0)
            p.inc(ii, "dve", 1)
        p.on("vector", zpad)
        p.stage()
        def conv(e, m=m):
            i1 = e.tensor_scalar(out=UB[:, :], in0=XR[:, 3:3+TF], scalar1=g['CW'][:, 4*m+3:4*m+4],
                                 scalar2=None, op0=OP.mult)
            p.inc(i1, "dve", 1); p.ser(e, "dve")
            for kk in range(3):
                ii = e.scalar_tensor_tensor(out=UB[:, :], in0=XR[:, kk:kk+TF],
                                            scalar=g['CW'][:, 4*m+kk:4*m+kk+1], in1=UB[:, :],
                                            op0=OP.mult, op1=OP.add)
                p.inc(ii, "dve", 1); p.ser(e, "dve")
        p.on("vector", conv)
        p.stage()
        def usil(e, m=m):
            ii = e.activation(out=UB[:, :], in_=UB[:, :], func=AF.Silu, bias=g['VM3'][:, 3*m+1:3*m+2])
            p.inc(ii, "act", 1)
        p.on("scalar", usil)
        p.stage()
        def du(e, m=m):
            ii = e.tensor_tensor(out=XR[:, 0:TF], in0=g['EV'][:, 0:TF], in1=UB[:, :], op=OP.mult)
            p.inc(ii, "dve", 1)
        p.on("vector", du)
        p.stage()
        def spdu(e, m=m):
            dma(p, e, "dS", g['ubuf'].ap()[m*128:(m+1)*128, :], UB[:, :])
            dma(p, e, "dS", g['dubuf'].ap()[m*128:(m+1)*128, :], XR[:, 0:TF])
        p.on("sync", spdu)
    p.stage()
    def lda(e):
        for m in range(MCT):
            dma(p, e, "dS", g['AT'][:, m*16:(m+1)*16], g['A_log'][m*128:(m+1)*128, :])
    p.on("sync", lda)
    p.stage()
    def mka(e):
        i1 = e.activation(out=g['AT'][:, :], in_=g['AT'][:, :], func=AF.Exp)
        p.inc(i1, "act", 1)
    p.on("scalar", mka)
    p.stage()
    def nega(e):
        ii = e.tensor_scalar(out=g['AT'][:, :], in0=g['AT'][:, :], scalar1=-1.0, scalar2=None, op0=OP.mult)
        p.inc(ii, "dve", 1)
    p.on("vector", nega)
    for m in range(MCT):
        p.stage()
        def lds(e, m=m):
            ii = e.dma_start(out=g['EV'][:, 0:TF], in_=g['dbuf'].ap()[m*128:(m+1)*128, :])
            p.inc(ii, "dS", 16)
            ii = e.dma_start(out=XR[:, 0:TF], in_=g['dubuf'].ap()[m*128:(m+1)*128, :])
            p.inc(ii, "dS", 16)
        p.on("sync", lds)
        for n in range(16):
            p.stage()
            def brp(e, m=m, n=n):
                src = g['bbuf'].ap()
                a = AP(src.tensor, src.offset + (16*m+n)*TF, [[0, 128], [1, TF]])
                dma(p, e, "dA", g['BRP'][:, :], a)
                srcc = g['cbuf'].ap()
                a2 = AP(srcc.tensor, srcc.offset + (128*m+n)*TF, [[16*TF, 8], [0, 16], [1, TF]])
                dma(p, e, "dA", CRP[:, :], a2)
            p.on("scalar", brp)
            p.stage()
            def dab(e, m=m, n=n):
                ii = e.activation(out=DAB[:, :], in_=g['EV'][:, 0:TF], func=AF.Exp,
                                  scale=g['AT'][:, m*16+n:m*16+n+1])
                p.inc(ii, "act", 1)
            p.on("scalar", dab)
            p.stage()
            def dub(e, m=m, n=n):
                ii = e.tensor_tensor(out=SSB[:, :], in0=XR[:, 0:TF], in1=g['BRP'][:, :], op=OP.mult)
                p.inc(ii, "dve", 1)
                p.ser(e, "dve")
                ii2 = e.tensor_tensor_scan(out=g['YTB'][:, :], data0=DAB[:, :], data1=SSB[:, :],
                                           initial=0.0, op0=OP.mult, op1=OP.add)
                p.inc(ii2, "dve", 1)
                p.ser(e, "dve")
                ii3 = e.tensor_tensor(out=g['YTB'][:, :], in0=g['YTB'][:, :], in1=CRP[:, :], op=OP.mult)
                p.inc(ii3, "dve", 1)
            p.on("vector", dub)
            p.stage()
            def yac(e, m=m, n=n):
                for q in range(4):
                    mm(p, e, PS[3+q][:, :], g['identbf'][:, :], g['YTB'][:, q*512:(q+1)*512],
                       start=(n==0), stop=(n==15), inc=True)
            p.on("tensor", yac)
        p.stage()
        def ldz(e, m=m):
            ii = e.dma_start(out=DAB[:, 0:TF], in_=g['zbuf'].ap()[m*128:(m+1)*128, :])
            p.inc(ii, "dS", 16)
            ii2 = e.dma_start(out=XR[:, 0:TF], in_=g['ubuf'].ap()[m*128:(m+1)*128, :])
            p.inc(ii2, "dS", 16)
        p.on("sync", ldz)
        p.stage()
        def zsil(e):
            ii = e.activation(out=DAB[:, :], in_=DAB[:, :], func=AF.Silu)
            p.inc(ii, "act", 1)
        p.on("scalar", zsil)
        p.stage()
        def ygt(e, m=m):
            for q in range(4):
                i1 = e.scalar_tensor_tensor(out=YG[:, q*512:(q+1)*512], in0=XR[:, q*512:(q+1)*512],
                                            scalar=g['VM3'][:, 3*m+2:3*m+3], in1=PS[3+q][:, :],
                                            op0=OP.mult, op1=OP.add)
            p.inc(i1, "dve", 1); p.ser(e, "dve")
            ii = e.tensor_tensor(out=YG[:, :], in0=YG[:, :], in1=DAB[:, :], op=OP.mult)
            p.inc(ii, "dve", 1)
        p.on("vector", ygt)
        p.stage()
        def spy(e, m=m):
            dma(p, e, "dS", g['ygbuf'].ap()[m*128:(m+1)*128, :], YG[:, :])
        p.on("sync", spy)
    YGQ = g['YGQ']
    for q in range(4):
        p.stage()
        def ldyq(e, q=q):
            for k in range(MCT):
                dma(p, e, "dS", YGQ[k][:, :], g['ygbuf'].ap()[k*128:(k+1)*128, q*512:(q+1)*512])
        p.on("sync", ldyq)
        for mo in range(16):
            p.stage()
            def ldwo(e, mo=mo):
                for k in range(MCT):
                    dma(p, e, "dG", WM[k][:, :], g['WoutT'][k*128:(k+1)*128, mo*128:(mo+1)*128])
            p.on("gpsimd", ldwo)
            p.stage()
            def omm(e, q=q, mo=mo):
                for k in range(MCT):
                    mm(p, e, PS[mo % 2][:, :], WM[k][:, :], YGQ[k][:, :],
                       start=(k==0), stop=(k==MCT-1), inc=(k==MCT-1))
            p.on("tensor", omm)
            p.stage()
            def oev(e, q=q, mo=mo):
                ii = e.tensor_copy(out=SCRF[mo % 2][:, :], in_=PS[mo % 2][:, :])
                p.inc(ii, "dve", 1)
            p.on("vector", oev)
            p.stage()
            def osp(e, q=q, mo=mo):
                dma(p, e, "dS", outsum[mo*128:(mo+1)*128, q*512:(q+1)*512], SCRF[mo % 2][:, :])
            p.on("sync", osp)

"""Main builder: allocations + phase sequencing."""

def _fin_copy(p, g, nc, din, xout, st):
    with ExitStack() as s5:
        XT = s5.enter_context(nc.sbuf_tensor("xf5", [128, 2048], F32))
        for i in range(NCT):
            p.stage()
            def ld(e, i=i):
                dma(p, e, "dS", XT[:, :], din["x_fm"][i*128:(i+1)*128, :])
            p.on("sync", ld)
            p.stage()
            def stx(e, i=i):
                dma(p, e, "dS", xout[i*128:(i+1)*128, :], XT[:, :])
            p.on("sync", stx)
    p.finish()
    return nc, din

GROUPS = [[0, 1], [2, 3], [4, 5], [6, 7]]

def declare_attn_inputs(nc):
    d = {}
    def di(name, shape, dt=F32):
        d[name] = nc.dram_tensor(name, shape, dt, kind="ExternalInput").ap()
    di("x_fm", [2048, 2048])
    for n in ("W_r_h", "W_k_h", "W_v_h"): di(n, [2048, 2048])
    di("W_o_h", [2048, 2048])
    di("w1", [2048, 64]); di("a1", [2048, 64])
    di("w2_h", [64, 2048]); di("a2_h", [64, 2048])
    di("g1", [2048, 128]); di("g2_h", [128, 2048])
    di("vecA", [2048, 6]); di("vecH", [2048, 8])
    di("ident_f", [128, 128]); di("mSU_f", [128, 128]); di("mSL_f", [128, 128]); di("mUI_f", [128, 128])
    di("ones_f", [128, 128]); di("on2_f", [128, 2]); di("sel2_f", [2, 128])
    di("on1_f", [128, 1]); di("on1r_f", [1, 128])
    return d

def declare_attn_dram(nc):
    g = {}
    def dr(name, shape, dt):
        g[name] = nc.dram_tensor(name, shape, dt, kind="Internal")
    dr("lnbuf", [2048, 2048], BF16)
    for n in ("sR", "sKraw", "sK2", "sV", "sKK", "sB_", "sAG", "sG", "sOACC"):
        dr(n, [2048, 2048], BF16)
    dr("sLDP", [2048, 2048], F32)
    dr("sGT", [1024, 2048], BF16); dr("sRAT", [1024, 4096], BF16)
    dr("sYV", [2048, 2048], BF16); dr("sU", [1024, 2048], BF16)
    dr("attnsum", [2048, 2048], F32)
    return g

def alloc_persistent(nc, st, g, din):
    sb = nc.sbuf_tensor
    g['PS'] = [st.enter_context(nc.psum_tensor(f"ps{i}", [128, 512], F32)) for i in range(7)]
    g['PST'] = st.enter_context(nc.psum_tensor("pst", [128, 512], BF16))
    g['SCR'] = [st.enter_context(sb(f"scr{i}", [128, 512], BF16)) for i in range(4)]
    g['SCRF'] = [st.enter_context(sb(f"scrf{i}", [128, 512], F32)) for i in range(2)]
    g['VEC'] = [st.enter_context(sb(f"vec{i}", [128, 6], F32)) for i in range(NCT)]
    g['VECH'] = [st.enter_context(sb(f"vech{i}", [128, 8], F32)) for i in range(16)]
    for n, shp, dt in (("identbf", [128, 128], BF16), ("mSU", [128, 128], BF16),
                       ("mSL", [128, 128], BF16), ("mUI", [128, 128], BF16),
                       ("onesbf", [128, 128], BF16), ("on2", [128, 2], BF16),
                       ("sel2f", [2, 128], F32), ("ones128_1", [128, 1], BF16),
                       ("ones1_128", [1, 128], F32), ("negw0", [128, 16], F32),
                       ("mka", [128, 16], F32), ("c_one", [128, 1], F32),
                       ("c_neghalf", [128, 1], F32), ("eps1", [1, 1], F32)):
        g[n] = st.enter_context(sb(n, shp, dt))
    g['ident64'] = g['identbf']
    g['groups'] = GROUPS

def emit_const_load(p, g, din):
    p.stage()
    def ldc(e):
        for nm_in, nm_sb in (("ident_f", "identbf"), ("mSU_f", "mSU"), ("mSL_f", "mSL"),
                             ("mUI_f", "mUI"), ("ones_f", "onesbf"), ("on2_f", "on2"),
                             ("on1_f", "ones128_1")):
            dma(p, e, "dG", g[nm_sb][:, :], din[nm_in])
    p.on("gpsimd", ldc)
    p.stage()
    def ldf(e):
        dma(p, e, "dS", g['sel2f'][:, :], din["sel2_f"])
        dma(p, e, "dS", g['ones1_128'][:, :], din["on1r_f"])
        for i in range(NCT):
            dma(p, e, "dS", g['VEC'][i][:, :], din["vecA"][i*128:(i+1)*128, :])
        for i in range(16):
            dma(p, e, "dS", g['VECH'][i][:, :], din["vecH"][i*128:(i+1)*128, :])
    p.on("sync", ldf)
    p.stage()
    def msets(e):
        e.memset(g['eps1'][:, :], 1e-5)
        e.memset(g['c_one'][:, :], 1.0)
        ii = e.memset(g['c_neghalf'][:, :], -0.5)
        p.inc(ii, "dve", 1)
        for m in range(16):
            e.tensor_scalar(out=g['negw0'][:, m:m+1], in0=g['VECH'][m][:, 0:1],
                            scalar1=-1.0, scalar2=None, op0=OP.mult)
            ii = e.tensor_scalar(out=g['mka'][:, m:m+1], in0=g['VECH'][m][:, 3:4],
                                 scalar1=-1.0, scalar2=1.0, op0=OP.mult, op1=OP.add)
        p.inc(ii, "dve", 1)
    p.on("vector", msets)

def emit_ln_phase(p, g, src_dram, out_dram, add_dram=None, writeback=None, add_bf16=False):
    nc = p.nc
    with ExitStack() as s:
        g['XB'] = [s.enter_context(nc.sbuf_tensor(uname(f"lnxb{i}"), [128, 2048], F32)) for i in range(2)]
        g['mean_sb'] = s.enter_context(nc.sbuf_tensor(uname("mean_sb"), [128, 2048], F32))
        g['rstd_sb'] = s.enter_context(nc.sbuf_tensor(uname("rstd_sb"), [128, 2048], F32))
        g['rows'] = [s.enter_context(nc.sbuf_tensor(uname(f"rows{i}"), [1, 2048], F32)) for i in range(3)]
        g['TB'] = [s.enter_context(nc.sbuf_tensor(uname(f"lntb{i}"), [128, 2048], BF16)) for i in range(2)]
        g['SQB'] = s.enter_context(nc.sbuf_tensor(uname("lnsqb"), [128, 2048], BF16))
        if add_dram is not None:
            g['ADB'] = [s.enter_context(nc.sbuf_tensor(uname(f"lnad{i}"), [128, 2048],
                                                       BF16 if add_bf16 else F32)) for i in range(2)]
        emit_ln_stream(p, g, src_dram, out_dram, add_dram=add_dram, writeback=writeback)

def declare_ffn_inputs(nc, d=None):
    d = d if d is not None else {}
    def di(name, shape, dt=F32):
        d[name] = nc.dram_tensor(name, shape, dt, kind="ExternalInput").ap()
    di("gate_wT", [2048, 8]); di("se_gT", [2048, 1])
    di("wgT", [8*2048, 1024]); di("wuT", [8*2048, 1024]); di("wdT", [8*1024, 2048])
    di("shgT", [2048, 2048]); di("shuT", [2048, 2048]); di("shdT", [2048, 2048])
    return d

def alloc_ffn(nc, s, g):
    sb = nc.sbuf_tensor
    g['LNT'] = [s.enter_context(sb(uname(f"flnt{i}"), [128, 1024], BF16)) for i in range(NCT)]
    g['WF'] = [s.enter_context(sb(uname(f"fwf{i}"), [128, 2048], BF16)) for i in range(8)]
    g['HP'] = [s.enter_context(sb(uname(f"fhp{i}"), [128, 1024], BF16)) for i in range(8)]
    g['ROUT'] = [s.enter_context(sb(uname(f"frt{i}"), [128, 1024], BF16)) for i in range(NCT)]
    g['GWB'] = [s.enter_context(sb(uname(f"fgw{i}"), [128, 9], BF16)) for i in range(NCT)]
    g['COMBR'] = [s.enter_context(sb(uname(f"fcb{i}"), [1, 1024], F32)) for i in range(8)]
    g['CB8'] = s.enter_context(sb(uname("fcb8"), [8, 1024], F32))
    for n, shp, dt in (("SCORES", [8, 1024], F32), ("SGR", [1, 1024], F32),
                       ("SGB", [128, 1024], BF16), ("CMB", [128, 1024], BF16),
                       ("RT", [128, 8], F32), ("RM1", [128, 1], F32), ("RM2", [128, 1], F32),
                       ("W1", [128, 1], F32), ("W2", [128, 1], F32), ("DD", [128, 1], F32),
                       ("MK1", [128, 8], F32), ("MK2", [128, 8], F32), ("SC2", [128, 8], F32),
                       ("ident_f_sb", [128, 128], F32)):
        g[n] = s.enter_context(sb(uname("f_" + n), shp, dt))

def declare_mamba_inputs(nc, d=None):
    d = d if d is not None else {}
    def di(name, shape, dt=F32):
        d[name] = nc.dram_tensor(name, shape, dt, kind="ExternalInput").ap()
    di("WinT", [2048, 9344]); di("WdtT", [128, 4096]); di("WoutT", [4096, 2048])
    di("A_log", [4096, 16]); di("conv_w", [4096, 4]); di("vecM", [4096, 3])
    return d

def declare_mamba_dram(nc, g):
    def dr(name, shape, dt):
        g[name] = nc.dram_tensor(name, shape, dt, kind="Internal")
    dr("zbuf", [4096, 2048], BF16); dr("xbr", [512, 2048], BF16)
    dr("bbuf", [512, 2048], BF16); dr("cbuf", [4096, 2048], BF16)
    dr("dbuf", [4096, 2048], BF16); dr("ubuf", [4096, 2048], BF16)
    dr("dubuf", [4096, 2048], BF16); dr("ygbuf", [4096, 2048], BF16)
    dr("mambasum", [2048, 2048], F32)

def alloc_mamba(nc, s, g):
    sb = nc.sbuf_tensor
    g['LN3T'] = [s.enter_context(sb(f"mln{i}", [128, 2048], BF16)) for i in range(NCT)]
    g['WM'] = [s.enter_context(sb(f"mwm{i}", [128, 128], BF16)) for i in range(32)]
    for n, shp, dt in (("EV", [128, 2048], BF16), ("DTL", [128, 2048], BF16),
                       ("XR", [128, 2051], BF16), ("UB", [128, 2048], BF16),
                       ("DAB", [128, 2048], BF16), ("SSB", [128, 2048], BF16),
                       ("YTB", [128, 2048], BF16), ("BRP", [128, 2048], BF16),
                       ("CRP", [128, 2048], BF16), ("YG", [128, 2048], BF16),
                       ("AT", [128, 512], F32), ("CW", [128, 128], F32),
                       ("VM3", [128, 96], F32), ("BDT2", [128, 32], F32)):
        g[n] = s.enter_context(sb("m_" + n, shp, dt))
    g['YGQ'] = [s.enter_context(sb(f"myq{i}", [128, 512], BF16)) for i in range(32)]

def build_full(nc):
    din = declare_attn_inputs(nc)
    declare_ffn_inputs(nc, din)
    declare_mamba_inputs(nc, din)
    g = declare_attn_dram(nc)
    declare_mamba_dram(nc, g)
    g['ffnsum'] = nc.dram_tensor("ffnsum", [2048, 2048], BF16, kind="Internal")
    g['combd'] = nc.dram_tensor("combd", [8, 1024], F32, kind="Internal")
    g['xres1'] = nc.dram_tensor("xres1", [2048, 2048], F32, kind="Internal")
    g['xres2'] = nc.dram_tensor("xres2", [2048, 2048], F32, kind="Internal")
    xout = nc.dram_tensor("xout", [2048, 2048], F32, kind="ExternalOutput").ap()
    with ExitStack() as st:
        with nc.Block() as block:
            p = P(nc, st, block)
            alloc_persistent(nc, st, g, din)
            for k in ("W_r_h","W_k_h","W_v_h","W_o_h","w1","w2_h","a1","a2_h","g1","g2_h",
                      "gate_wT","se_gT","wgT","wuT","wdT","shgT","shuT","shdT",
                      "WinT","WdtT","WoutT","A_log","conv_w","vecM"):
                g[k] = din[k]
            emit_const_load(p, g, din)
            emit_ln_phase(p, g, din["x_fm"], g['lnbuf'].ap())
            with ExitStack() as s1:
                g['MQ'] = [s1.enter_context(nc.sbuf_tensor(f"mq{i}", [128, 512], BF16)) for i in range(NCT)]
                g['WT'] = [s1.enter_context(nc.sbuf_tensor(f"wt{i}", [128, 2048], BF16)) for i in range(NCT)]
                g['LN1F'] = [s1.enter_context(nc.sbuf_tensor(f"lnf{i}", [128, 2049], BF16)) for i in range(NCT)]
                g['SGT'] = s1.enter_context(nc.sbuf_tensor("sgt", [128, 2048], BF16))
                g['TW'] = s1.enter_context(nc.sbuf_tensor("tw", [64, 2048], BF16))
                g['WT2'] = [s1.enter_context(nc.sbuf_tensor("wtb2", [128, 2048], BF16))]
                emit_attn_p1(p, g)
            with ExitStack() as s2:
                for n, shp, dt in (("KRW", [128, 2048], BF16), ("AGB", [128, 2048], BF16),
                                   ("K2B", [128, 2048], BF16), ("SQT", [128, 2048], BF16),
                                   ("RN2", [2, 2048], F32), ("RR2", [2, 2048], F32)):
                    g[n] = s2.enter_context(nc.sbuf_tensor(n, shp, dt))
                emit_attn_p2(p, g)
            with ExitStack() as s3:
                for n, shp, dt in (("RHB", [128, 2048], BF16), ("KHB", [128, 2048], BF16),
                                   ("BHB", [128, 2048], BF16), ("ANB", [128, 2048], BF16),
                                   ("VB", [128, 2048], BF16), ("EDB", [128, 2048], BF16),
                                   ("EIB", [128, 2048], BF16), ("TMB", [128, 2048], BF16),
                                   ("LDPF", [128, 2048], F32), ("CLDF", [128, 2048], F32),
                                   ("DLC", [128, 16], F32),
                                   ("VTt", [128, 2048], BF16), ("KTTt", [128, 2048], BF16),
                                   ("BTTt", [128, 2048], BF16),
                                   ("UPc", [128, 4096], BF16), ("LOc", [128, 4096], BF16),
                                   ("UPn", [128, 4096], BF16), ("LOn", [128, 4096], BF16),
                                   ("MAKw", [128, 4096], BF16), ("MRBw", [128, 4096], BF16),
                                   ("MRKw", [128, 4096], BF16), ("HC", [128, 4096], BF16),
                                   ("GTw", [64, 2048], BF16), ("RATw", [64, 4096], BF16),
                                   ("YVw", [128, 2048], BF16), ("Uw", [64, 2048], BF16),
                                   ("DIAG", [64, 2048], BF16)):
                    g[n] = s3.enter_context(nc.sbuf_tensor(n, shp, dt))
                emit_attn_p3(p, g)
            with ExitStack() as s4:
                for n, shp, dt in (("ZA", [64, 2048], BF16), ("ZB", [64, 2048], BF16),
                                   ("GTld", [64, 2048], BF16), ("RATld", [64, 4096], BF16),
                                   ("YVld", [128, 2048], BF16), ("Uld", [64, 2048], BF16),
                                   ("YSB", [128, 2048], BF16), ("TPGN", [128, 512], BF16)):
                    g[n] = s4.enter_context(nc.sbuf_tensor(n, shp, dt))
                g['OACCB'] = [s4.enter_context(nc.sbuf_tensor(f"oac{i}", [128, 2048], BF16)) for i in range(16)]
                g['WG'] = [s4.enter_context(nc.sbuf_tensor(f"wg{i}", [128, 2048], BF16)) for i in range(16)]
                g['WOT'] = g['WG']
                emit_attn_p4(p, g)
            emit_ln_phase(p, g, din["x_fm"], g['lnbuf'].ap(),
                          add_dram=g['attnsum'].ap(), writeback=g['xres1'].ap())
            with ExitStack() as sf:
                alloc_ffn(nc, sf, g)
                p.stage()
                def ldid(e):
                    dma(p, e, "dS", g['ident_f_sb'][:, :], din["ident_f"])
                p.on("sync", ldid)
                emit_ffn(p, g, g['lnbuf'].ap(), g['ffnsum'].ap())
            emit_ln_phase(p, g, g['xres1'].ap(), g['lnbuf'].ap(),
                          add_dram=g['ffnsum'].ap(), writeback=g['xres2'].ap(), add_bf16=True)
            with ExitStack() as sm:
                alloc_mamba(nc, sm, g)
                emit_mamba(p, g, g['lnbuf'].ap(), g['mambasum'].ap())
            emit_ln_phase(p, g, g['xres2'].ap(), g['lnbuf'].ap(),
                          add_dram=g['mambasum'].ap(), writeback=g['xres1'].ap())
            with ExitStack() as sf2:
                alloc_ffn2 = alloc_ffn
                with ExitStack() as sff:
                    pass
                alloc_ffn(nc, sf2, g)
                p.stage()
                def ldid2(e):
                    dma(p, e, "dS", g['ident_f_sb'][:, :], din["ident_f"])
                p.on("sync", ldid2)
                emit_ffn(p, g, g['lnbuf'].ap(), g['ffnsum'].ap())
            with ExitStack() as s5:
                XT = s5.enter_context(nc.sbuf_tensor("xz6", [128, 2048], F32))
                AT2 = s5.enter_context(nc.sbuf_tensor("az6", [128, 2048], BF16))
                for i in range(NCT):
                    p.stage()
                    def ld(e, i=i):
                        dma(p, e, "dS", XT[:, :], g['xres1'].ap()[i*128:(i+1)*128, :])
                        dma(p, e, "dS", AT2[:, :], g['ffnsum'].ap()[i*128:(i+1)*128, :])
                    p.on("sync", ld)
                    p.stage()
                    def ad(e, i=i):
                        ii = e.tensor_tensor(out=XT[:, :], in0=XT[:, :], in1=AT2[:, :], op=OP.add)
                        p.inc(ii, "dve", 1)
                    p.on("vector", ad)
                    p.stage()
                    def stx(e, i=i):
                        dma(p, e, "dS", xout[i*128:(i+1)*128, :], XT[:, :])
                    p.on("sync", stx)
            p.finish()
    return nc, din


_CACHE = {}

def _consts():
    return {
        "ident_f": np.eye(128, dtype=np.float32),
        "mSU_f": np.triu(np.ones((128, 128), np.float32), 1),
        "mSL_f": np.tril(np.ones((128, 128), np.float32), -1),
        "mUI_f": np.triu(np.ones((128, 128), np.float32), 0),
        "ones_f": np.ones((128, 128), np.float32),
        "on2_f": np.stack([np.r_[np.ones(64), np.zeros(64)],
                           np.r_[np.zeros(64), np.ones(64)]], axis=1).astype(np.float32),
        "sel2_f": np.stack([np.r_[np.ones(64), np.zeros(64)],
                            np.r_[np.zeros(64), np.ones(64)]], axis=0).astype(np.float32),
        "on1_f": np.ones((128, 1), np.float32),
        "on1r_f": np.ones((1, 128), np.float32),
    }

def kernel(**ins):
    f32 = np.float32
    A = lambda v: np.ascontiguousarray(np.asarray(v, f32))
    x = A(ins["L0_attn_a_x"])
    r1 = lambda v: np.asarray(v, f32).reshape(-1)
    consts = _consts()
    vecA = np.stack([r1(ins["L0_attn_a_x_r"]), r1(ins["L0_attn_a_x_w"]), r1(ins["L0_attn_a_x_k"]),
                     r1(ins["L0_attn_a_x_v"]), r1(ins["L0_attn_a_x_a"]), r1(ins["L0_attn_a_x_g"])], axis=1)
    vecH = np.stack([r1(ins["L0_attn_a_w0"]), r1(ins["L0_attn_a_a0"]), r1(ins["L0_attn_a_k_k"]),
                     r1(ins["L0_attn_a_k_a"]), r1(ins["L0_attn_a_gn_scale"]), r1(ins["L0_attn_a_gn_offset"]),
                     r1(ins["L0_attn_a_r_k"]), np.zeros(2048, f32)], axis=1)
    wg = np.asarray(ins["L0_ffn_w_gate"], f32)
    wu = np.asarray(ins["L0_ffn_w_up"], f32)
    wd = np.asarray(ins["L0_ffn_w_down"], f32)
    common = {
        "W_r_h": A(ins["L0_attn_a_W_r"]), "W_k_h": A(ins["L0_attn_a_W_k"]),
        "W_v_h": A(ins["L0_attn_a_W_v"]), "W_o_h": A(ins["L0_attn_a_W_o"]),
        "w1": A(ins["L0_attn_a_w1"]), "w2_h": A(ins["L0_attn_a_w2"]),
        "a1": A(ins["L0_attn_a_a1"]), "a2_h": A(ins["L0_attn_a_a2"]),
        "g1": A(ins["L0_attn_a_g1"]), "g2_h": A(ins["L0_attn_a_g2"]),
        "vecA": A(vecA), "vecH": A(vecH),
        "gate_wT": A(np.asarray(ins["L0_ffn_gate_w"], f32).T),
        "se_gT": A(np.asarray(ins["L0_ffn_se_gate_w"], f32).T),
        "wgT": A(wg.transpose(0, 2, 1).reshape(8*2048, 1024)),
        "wuT": A(wu.transpose(0, 2, 1).reshape(8*2048, 1024)),
        "wdT": A(wd.transpose(0, 2, 1).reshape(8*1024, 2048)),
        "shgT": A(np.asarray(ins["L0_ffn_sh_gate"], f32).T),
        "shuT": A(np.asarray(ins["L0_ffn_sh_up"], f32).T),
        "shdT": A(np.asarray(ins["L0_ffn_sh_down"], f32).T),
        "WinT": A(np.asarray(ins["L0_attn_b_W_in"], f32).T),
        "WdtT": A(np.asarray(ins["L0_attn_b_W_dt"], f32).T),
        "WoutT": A(np.asarray(ins["L0_attn_b_W_out"], f32).T),
        "A_log": A(ins["L0_attn_b_A_log"]),
        "conv_w": A(np.asarray(ins["L0_attn_b_conv_w"], f32)[:, 0, :]),
        "vecM": A(np.stack([r1(ins["L0_attn_b_b_dt"]), r1(ins["L0_attn_b_conv_b"]),
                            r1(ins["L0_attn_b_D_param"])], axis=1)),
    }
    common.update(consts)
    in_maps = []
    for c in range(8):
        d = dict(common)
        d["x_fm"] = A(x[c % 4].T)
        in_maps.append(d)
    if "nc" not in _CACHE:
        nc = bass.Bass("TRN2", target_bir_lowering=False)
        nc, _ = build_full(nc)
        _CACHE["nc"] = nc
    res = run_bass_kernel_spmd(_CACHE["nc"], in_maps, core_ids=list(range(8)))
    out = np.empty((4, 2048, 2048), f32)
    for b in range(4):
        out[b] = res.results[b]["xout"].T
    return out
